# version 04
# baseline (speedup 1.0000x reference)
import os
import numpy as np
import ml_dtypes
from contextlib import ExitStack

KCUT = int(os.environ.get("KCUT", "99"))

class _Cut(Exception):
    pass

import concourse.bass as bass
import concourse.mybir as mybir
from concourse.bass_utils import run_bass_kernel_spmd

F32 = mybir.dt.float32
BF16 = mybir.dt.bfloat16
FP8 = mybir.dt.float8e4
U8 = mybir.dt.uint8
AF = mybir.ActivationFunctionType
ALU = mybir.AluOpType
AX = mybir.AxisListType

D = 2048
PRE = 512
NQ = 2048
NKV = 2560
HQ, HKV, DH = 16, 4, 128
IT = 5632
NBKV = 20
NBQ = 16
EPS1 = 1e-5
EPS2 = 1e-6
NEG = -1e30
ISQ = float(1.0 / np.sqrt(128.0))
GRPS = [(0, 1536), (1536, 3072), (3072, 4608), (4608, 5632)]

ENGS = ("sync", "gpsimd", "tensor", "vector", "scalar")


def build_program():
    nc = bass.Bass("TRN2", target_bir_lowering=False)

    dram = {}
    def din(name, shape, dt=F32):
        dram[name] = nc.dram_tensor(name, list(shape), dt, kind="ExternalInput").ap()
    din("xkv", (NKV, D))
    din("wq", (D, D)); din("wk", (PRE, D)); din("wv", (PRE, D)); din("wo", (D, D))
    din("wg", (IT, D)); din("wu", (IT, D)); din("wd", (D, IT))
    din("wmod", (D, 6 * D))
    din("cbt", (128, 16))
    din("qnrow", (1, 128)); din("knT", (128, 1))
    din("cosT", (128, NKV)); din("sinT", (128, NKV))
    din("kbT", (128, NBKV))
    din("maskadd", (128, 1024))
    din("identb", (128, 128), BF16); din("ident8", (128, 128), FP8)
    din("identf", (128, 128), F32)
    din("onescol", (128, 1), BF16); din("onesrow", (1, 128))
    out = nc.dram_tensor("out", [NQ, D], F32, kind="ExternalOutput").ap()
    t_sp = nc.dram_tensor("t_sp", [NQ, IT], F32, kind="Internal").ap()
    x2_sp = nc.dram_tensor("x2_sp", [NQ, D], F32, kind="Internal").ap()
    qx8_sp = nc.dram_tensor("qx8_sp", [NBQ, 128, 16, 128], FP8, kind="Internal").ap()
    bounce = nc.dram_tensor("bounce", [1, 8192], F32, kind="Internal").ap()
    rows_hbm = nc.dram_tensor("rows_hbm", [8, D], F32, kind="Internal").ap()

    cmds = {e: [] for e in ENGS}
    sems = {}
    C = {}
    hi = {e: {} for e in ENGS}
    stack = ExitStack()

    def new_sem(name):
        sems[name] = stack.enter_context(nc.semaphore(name))
        C[name] = 0

    def em(eng, f):
        cmds[eng].append(f)

    def W(eng, s, v):
        if v is None or v <= 0:
            return
        assert v <= C[s], f"wait-ahead: {eng} waits {s}>={v}, emitted {C[s]}"
        if hi[eng].get(s, 0) >= v:
            return
        hi[eng][s] = v
        em(eng, lambda E, s=s, v=v: E.wait_ge(sems[s], v))

    SER = {"vector": "vS", "scalar": "aS"}

    def op(eng, f, inc=None):
        s_auto = SER.get(eng)
        if s_auto is not None:
            W(eng, s_auto, C[s_auto])
            if inc is None:
                inc = (s_auto, 1)
            else:
                assert inc[0] == s_auto, (eng, inc)
        if inc is None:
            em(eng, f)
            return None
        s, n = inc
        C[s] += n
        em(eng, lambda E, f=f, s=s, n=n: f(E).then_inc(sems[s], n))
        return C[s]

    for s in ("tS", "vS", "aS"):
        new_sem(s)

    def cut(n, dump=None):
        if KCUT != n:
            return
        dbg = Slot("sync")
        if dump == "rows":
            ew = [("aS", C["aS"])]
            for sn, cv in list(C.items()):
                if sn.startswith("dq"):
                    ew.append((sn, cv))
            dbg.load(lambda: out[0:8, :], lambda: rows_hbm[0:8, :], extra_wait=ew)
        elif isinstance(dump, tuple) and dump[0] == "sb":
            ew = [("aS", C["aS"]), ("vS", C["vS"]), ("tS", C["tS"])]
            for sn, cv in list(C.items()):
                if sn.startswith("dq"):
                    ew.append((sn, cv))
            nmq, wd = dump[1], dump[2]
            dbg.load(lambda n=nmq, w=wd: out[0:128, 0:w],
                     lambda n=nmq, w=wd: B[n][:, 0:w], extra_wait=ew)
        elif dump == "x2":
            ew = [("vS", C["vS"])]
            for sn, cv in list(C.items()):
                if sn.startswith("dq"):
                    ew.append((sn, cv))
            for jb in range(16):
                dbg.load(lambda j=jb: out[j * 128:(j + 1) * 128, :],
                         lambda j=jb: x2_sp[j * 128:(j + 1) * 128, :],
                         extra_wait=ew if jb == 0 else None)
        em("sync", lambda E, d=dbg: E.wait_ge(sems[d.s], 16 * d.uses))
        _do_replay(nc, cmds, stack)
        e = _Cut()
        e.nc = nc
        raise e

    class Slot:
        _n = 0
        def __init__(self, queue):
            Slot._n += 1
            self.s = f"dq{Slot._n}"
            new_sem(self.s)
            self.queue = queue
            self.uses = 0
            self.freecond = None
        def load(self, dst_f, src_f, extra_wait=None):
            if self.freecond is not None:
                W(self.queue, *self.freecond)
            if extra_wait is not None:
                ews = extra_wait if isinstance(extra_wait, list) else [extra_wait]
                for ew in ews:
                    if ew is not None:
                        W(self.queue, *ew)
            self.uses += 1
            op(self.queue, lambda E, d=dst_f, s=src_f: E.dma_start(d(), s()),
               inc=(self.s, 16))
            return 16 * self.uses
        def wait(self, eng):
            W(eng, self.s, 16 * self.uses)
        def set_free(self, sem, val):
            self.freecond = (sem, val)

    B, CMS = {}, {}
    def alloc(name, shape, dt):
        assert name not in B, f"sbuf name reuse: {name}"
        cm = nc.sbuf_tensor("sb_" + name, list(shape), dt)
        B[name] = cm.__enter__()
        CMS[name] = cm
    reuse_bar = {}

    def free(*names):
        for n in names:
            CMS.pop(n).__exit__(None, None, None)
        for s, v in C.items():
            if v > 0:
                reuse_bar[s] = v

    def BAR():
        return [(s, v) for s, v in reuse_bar.items() if v > 0]

    def bw(eng):
        for s, v in reuse_bar.items():
            W(eng, s, v)

    ps = []
    for i in range(8):
        pcm = nc.psum_tensor(f"psb{i}", [128, 512], F32)
        ps.append(pcm.__enter__())
        stack.callback(pcm.__exit__, None, None, None)
    bank_guard = [None] * 8
    def bguard(i):
        if bank_guard[i] is not None:
            W("tensor", *bank_guard[i])
            bank_guard[i] = None
    def bread(i, sem):
        bank_guard[i] = (sem, C[sem])

    last_read = {}
    def rguard(writer_eng, name):
        if name in last_read:
            W(writer_eng, *last_read.pop(name))
    def rmark(name, sem):
        last_read[name] = (sem, C[sem])

    alloc("identb", (128, 128), BF16)
    alloc("ident8", (128, 128), FP8)
    alloc("identf", (128, 128), F32)
    alloc("onescol", (128, 1), BF16)
    alloc("onesrow", (1, 128), F32)
    alloc("qnrow", (1, 128), F32)
    alloc("qn1s", (1, 128), F32)
    alloc("knT", (128, 1), F32)
    alloc("kn1", (128, 1), F32)
    alloc("kbT", (128, NBKV), F32)
    alloc("maskadd", (128, 1024), F32)
    alloc("cbt", (128, 16), F32)
    alloc("siluc", (128, 16), BF16)
    alloc("silucF", (128, 16), F32)
    alloc("gacol", (128, 16), F32)
    alloc("gax", (128, 16), F32)
    alloc("gay", (128, 16), F32)
    alloc("gamsk", (128, 16), U8)
    alloc("rowA", (1, D), F32)
    alloc("rowB", (1, D), F32)
    alloc("rowC", (1, D), F32)
    alloc("bc0", (128, D), F32)
    alloc("bc1", (128, D), F32)
    alloc("rstd1", (128, NBKV), F32)
    alloc("nmr1", (128, NBKV), F32)
    alloc("rxf", (128, NBQ), F32)
    alloc("rta", (128, NBQ), F32)
    alloc("rtb", (128, NBQ), F32)
    alloc("rtf", (128, NBQ), F32)
    alloc("rtinv", (128, NBQ), F32)
    alloc("wmc", (128, 16), F32)
    alloc("wmr16", (16, 128), F32)
    alloc("s11", (128, 1), F32)
    alloc("s12", (128, 1), F32)
    alloc("s13", (128, 1), F32)
    alloc("s14", (128, 1), F32)
    alloc("hzd", (128, 1), F32)
    alloc("rstd2", (128, 1), F32)
    alloc("nmr2", (128, 1), F32)
    alloc("r1a", (1, 512), F32)
    alloc("r1b", (1, 512), F32)
    alloc("r1c", (1, 512), F32)
    alloc("scr1", (128, D), F32)
    alloc("scr2", (128, D), F32)
    alloc("dump", (128, D), BF16)
    alloc("kC", (128, HKV, NKV), BF16)
    alloc("vT", (128, NBKV, PRE), BF16)

    cst = Slot("sync")
    for nm in ("identb", "ident8", "identf", "onescol", "onesrow", "qnrow",
               "knT", "kbT", "maskadd", "cbt"):
        cst.load(lambda n=nm: B[n][:], lambda n=nm: dram[n][:, :])
    cst.wait("vector")
    op("vector", lambda E: E.tensor_scalar(B["qn1s"][:], B["qnrow"][:], 1.0, ISQ, ALU.add, ALU.mult))
    op("vector", lambda E: E.tensor_scalar(B["kn1"][:], B["knT"][:], 1.0, None, ALU.add))
    op("vector", lambda E: E.memset(B["wmc"][:], 1.0), inc=("vS", 1))
    consts_v = C["vS"]
    cst.wait("scalar")
    op("scalar", lambda E: E.activation(B["siluc"][:], B["cbt"][:], AF.Silu), inc=("aS", 1))
    op("scalar", lambda E: E.activation(B["silucF"][:], B["cbt"][:], AF.Silu), inc=("aS", 1))
    siluc_a = C["aS"]

    rowld = Slot("sync")
    rowst = Slot("sync")

    def transpose16(src_f, dst_f, ident, banks=(4, 5), mark=None):
        for c4 in range(4):
            bk = banks[c4 % 2]
            bguard(bk)
            for ci in range(4):
                cht = c4 * 4 + ci
                op("tensor", lambda E, c=cht, i=ci, b=bk, s=src_f, id=ident: E.matmul(
                    ps[b][:, i * 128:(i + 1) * 128], s(c), B[id][:],
                    start=True, stop=True), inc=("tS", 1) if ci == 3 else None)
            if mark:
                rmark(mark, "tS")
            W("scalar", "tS", C["tS"])
            op("scalar", lambda E, c4=c4, b=bk, d=dst_f: E.copy(
                d(c4), ps[b][:, :].rearrange("p (a t) -> p a t", a=4)), inc=("aS", 1))
            bread(bk, "aS")
        return C["aS"]

    def bcast(dst, slot_idx, rowbuf="rowC"):
        rguard("sync", rowbuf)
        rowld.load(lambda r=rowbuf: B[r][:], lambda si=slot_idx: rows_hbm[si:si + 1, :],
                   extra_wait=(rowst.s, 16 * rowst.uses))
        rowld.wait("tensor")
        rguard("scalar", dst)
        for q in range(4):
            bk = 6 + (q % 2)
            bguard(bk)
            op("tensor", lambda E, q=q, r=rowbuf, b=bk: E.matmul(
                ps[b][:, :], B["onesrow"][:], B[r][0:1, q * 512:(q + 1) * 512],
                start=True, stop=True), inc=("tS", 1))
            rmark(rowbuf, "tS")
            W("scalar", "tS", C["tS"])
            op("scalar", lambda E, q=q, d=dst, b=bk: E.copy(B[d][:, q * 512:(q + 1) * 512], ps[b][:, :]),
               inc=("aS", 1))
            bread(bk, "aS")
        return C["aS"]

    alloc("mskq", (1, D), U8)
    alloc("wmbuf", (128, 2, D), BF16)
    wmslot = [Slot("gpsimd"), Slot("gpsimd")]
    W("tensor", "aS", siluc_a)
    for cg in range(6):
        for q in range(4):
            bguard(q)
        for cht in range(16):
            sl = wmslot[cht % 2]
            sl.load(lambda c=cht: B["wmbuf"][:, c % 2, :],
                    lambda c=cht, g=cg: dram["wmod"][c * 128:(c + 1) * 128, g * 2048:(g + 1) * 2048])
            sl.wait("tensor")
            for q in range(4):
                op("tensor", lambda E, c=cht, q=q: E.matmul(
                    ps[q][0:1, :], B["siluc"][:, c:c + 1], B["wmbuf"][:, c % 2, q * 512:(q + 1) * 512],
                    start=(c == 0), stop=(c == 15)),
                   inc=("tS", 1) if q == 3 else None)
            sl.set_free("tS", C["tS"])
        W("scalar", "tS", C["tS"])
        rguard("scalar", "rowA")
        for q in range(4):
            op("scalar", lambda E, q=q: E.copy(B["rowA"][0:1, q * 512:(q + 1) * 512], ps[q][0:1, :]),
               inc=("aS", 1))
            bread(q, "aS")
        rowst.load(lambda g=cg: rows_hbm[g:g + 1, :], lambda: B["rowA"][:],
                   extra_wait=("aS", C["aS"]))
        rmark("rowA", rowst.s)
    free("wmbuf")
    mod_st = 16 * rowst.uses

    def row_fix(slot_idx, kind):
        rguard("sync", "rowA")
        rowld.load(lambda: B["rowA"][:], lambda si=slot_idx: rows_hbm[si:si + 1, :],
                   extra_wait=(rowst.s, 16 * rowst.uses))
        rowld.wait("vector")
        if kind == "clamp":
            op("vector", lambda E: E.tensor_scalar(B["rowB"][:], B["rowA"][:], -1.0, None, ALU.mult))
            op("vector", lambda E: E.tensor_tensor(B["rowB"][:], B["rowA"][:], B["rowB"][:], ALU.max))
            op("vector", lambda E: E.tensor_scalar(B["rowB"][:], B["rowB"][:], 0.25, None, ALU.max))
            op("vector", lambda E: E.tensor_scalar(B["rowC"][:], B["rowA"][:], 0.0, None, ALU.is_ge))
            op("vector", lambda E: E.tensor_scalar(B["rowC"][:], B["rowC"][:], 2.0, -1.0, ALU.mult, ALU.add))
            op("vector", lambda E: E.tensor_tensor(B["rowA"][:], B["rowC"][:], B["rowB"][:], ALU.mult), inc=("vS", 1))
        else:
            op("vector", lambda E: E.tensor_scalar(B["rowA"][:], B["rowA"][:], 1.0, None, ALU.add), inc=("vS", 1))
        rmark("rowC", "vS"); rmark("rowB", "vS")
        rowst.load(lambda si=slot_idx: rows_hbm[si:si + 1, :], lambda: B["rowA"][:],
                   extra_wait=("vS", C["vS"]))
        rmark("rowA", rowst.s)
    alloc("wmfT", (128, 2, D), F32)
    alloc("wmHT", (128, 2, D), BF16)
    alloc("wmLT", (128, 2, D), BF16)
    alloc("sLb", (128, 16), BF16)
    bw("vector")
    W("vector", "aS", siluc_a)
    op("vector", lambda E: E.tensor_tensor(B["sLb"][:], B["silucF"][:], B["siluc"][:], ALU.subtract), inc=("vS", 1))
    sLb_v = C["vS"]
    wmfs = [Slot("sync"), Slot("sync")]
    wmhs = [Slot("gpsimd"), Slot("gpsimd")]
    mm_after2 = {0: 0, 1: 0}
    for term, (slot_idx, coloff) in enumerate([(2, 2 * D), (5, 5 * D)]):
        for q in range(4):
            bguard(q)
        for cht in range(16):
            pr = cht % 2
            fw = BAR() if (term == 0 and cht < 2) else None
            wmfs[pr].load(lambda c=cht, co=coloff: B["wmfT"][:, c % 2, :],
                          lambda c=cht, co=coloff: dram["wmod"][c * 128:(c + 1) * 128, co:co + D],
                          extra_wait=fw)
            wmhs[pr].load(lambda c=cht, co=coloff: B["wmHT"][:, c % 2, :],
                          lambda c=cht, co=coloff: dram["wmod"][c * 128:(c + 1) * 128, co:co + D],
                          extra_wait=fw)
            wmfs[pr].wait("vector"); wmhs[pr].wait("vector")
            W("vector", "tS", mm_after2[pr])
            op("vector", lambda E, c=cht: E.tensor_tensor(B["wmLT"][:, c % 2, :], B["wmfT"][:, c % 2, :], B["wmHT"][:, c % 2, :], ALU.subtract), inc=("vS", 1))
            wmfs[pr].set_free("vS", C["vS"])
            W("tensor", "vS", C["vS"])
            W("tensor", "vS", sLb_v)
            wmhs[pr].wait("tensor")
            first = (cht == 0)
            last = (cht == 15)
            for q in range(4):
                op("tensor", lambda E, c=cht, q=q, f=first: E.matmul(
                    ps[q][0:1, :], B["siluc"][:, c:c + 1], B["wmHT"][:, c % 2, q * 512:(q + 1) * 512],
                    start=f, stop=False))
                op("tensor", lambda E, c=cht, q=q: E.matmul(
                    ps[q][0:1, :], B["sLb"][:, c:c + 1], B["wmHT"][:, c % 2, q * 512:(q + 1) * 512],
                    start=False, stop=False))
                op("tensor", lambda E, c=cht, q=q: E.matmul(
                    ps[q][0:1, :], B["siluc"][:, c:c + 1], B["wmLT"][:, c % 2, q * 512:(q + 1) * 512],
                    start=False, stop=False))
                op("tensor", lambda E, c=cht, q=q, l=last: E.matmul(
                    ps[q][0:1, :], B["sLb"][:, c:c + 1], B["wmLT"][:, c % 2, q * 512:(q + 1) * 512],
                    start=False, stop=l), inc=("tS", 1) if q == 3 else None)
            wmhs[pr].set_free("tS", C["tS"])
            mm_after2[pr] = C["tS"]
        W("scalar", "tS", C["tS"])
        rguard("scalar", "rowA")
        for q in range(4):
            op("scalar", lambda E, q=q: E.copy(B["rowA"][0:1, q * 512:(q + 1) * 512], ps[q][0:1, :]), inc=("aS", 1))
            bread(q, "aS")
        rowst.load(lambda si=slot_idx: rows_hbm[si:si + 1, :], lambda: B["rowA"][:],
                   extra_wait=("aS", C["aS"]))
        rmark("rowA", rowst.s)
    free("sLb", "wmLT", "wmHT", "wmfT")
    if KCUT == 12:
        dbg = Slot("sync")
        ew = [("aS", C["aS"]), ("vS", C["vS"])]
        for sn, cv in list(C.items()):
            if sn.startswith("dq"):
                ew.append((sn, cv))
        dbg.load(lambda: out[144:152, 0:2048], lambda: rows_hbm[0:8, :], extra_wait=ew)
        em("sync", lambda E, d=dbg: E.wait_ge(sems[d.s], 16 * d.uses))
        _do_replay(nc, cmds, stack)
        e = _Cut(); e.nc = nc
        raise e
    row_fix(2, "clamp")
    row_fix(5, "clamp")
    row_fix(1, "add1")
    row_fix(4, "add1")
    free("mskq")

    bcast("bc0", 0)
    bcast("bc1", 1)
    cut(1, "rows")

    alloc("WkT", (128, 16, PRE), BF16)
    alloc("WvT", (128, 16, PRE), BF16)
    alloc("wnat", (128, 4, D), BF16)
    wks = Slot("gpsimd")
    def prep_kv(src, dstname):
        bw("scalar")
        rguard("gpsimd", "wnat")
        wks.load(lambda: B["wnat"][:, :, :],
                 lambda s=src: dram[s].rearrange("(r p) c -> p r c", p=128),
                 extra_wait=BAR())
        wks.wait("tensor")
        for rt in range(4):
            transpose16(lambda c, r=rt: B["wnat"][:, r, c * 128:(c + 1) * 128],
                        lambda c4, r=rt, d=dstname: B[d][:, c4 * 4:(c4 + 1) * 4, r * 128:(r + 1) * 128],
                        "identb", mark="wnat")
    prep_kv("wk", "WkT")
    prep_kv("wv", "WvT")
    free("wnat")

    alloc("cosF", (128, NKV), F32)
    alloc("sinF", (128, NKV), F32)
    cs = Slot("sync")
    cs.load(lambda: B["cosF"][:], lambda: dram["cosT"][:, :], extra_wait=BAR())
    cs.load(lambda: B["sinF"][:], lambda: dram["sinT"][:, :])
    alloc("xblk", (128, 2, D), F32)
    alloc("h1Tb", (128, 2, 16 * 128), BF16)
    alloc("k1", (128, 512), F32)
    alloc("ksw", (128, 512), F32)
    alloc("kc1", (128, 512), F32)
    alloc("kc2", (128, 512), F32)
    xs = [Slot("sync"), Slot("sync")]
    sw = Slot("gpsimd")

    def ln_adaln(xap, rstd_ap, nmr_ap, outap, outname=None):
        rguard("scalar", "scr2")
        op("scalar", lambda E, x=xap: E.activation(B["scr2"][:], x, AF.Square), inc=("aS", 1))
        sq_a = C["aS"]
        op("vector", lambda E, x=xap: E.tensor_reduce(B["s11"][:], x, AX.X, ALU.add))
        W("vector", "aS", sq_a)
        op("vector", lambda E: E.tensor_reduce(B["s13"][:], B["scr2"][:], AX.X, ALU.add))
        op("vector", lambda E: E.tensor_scalar(B["s12"][:], B["s11"][:], -1.0 / D, None, ALU.mult))
        op("vector", lambda E: E.tensor_scalar(B["s14"][:], B["s13"][:], 1.0 / D, EPS1, ALU.mult, ALU.add))
        op("vector", lambda E: E.tensor_tensor(B["s11"][:], B["s12"][:], B["s12"][:], ALU.mult))
        op("vector", lambda E: E.tensor_tensor(B["s13"][:], B["s14"][:], B["s11"][:], ALU.subtract), inc=("vS", 1))
        W("scalar", "vS", C["vS"])
        op("scalar", lambda E: E.activation(B["s14"][:], B["s13"][:], AF.Sqrt), inc=("aS", 1))
        W("vector", "aS", C["aS"])
        op("vector", lambda E, r=rstd_ap: E.reciprocal(r, B["s14"][:]))
        op("vector", lambda E, r=rstd_ap, n=nmr_ap: E.tensor_tensor(n, B["s12"][:], r, ALU.mult), inc=("vS", 1))
        W("vector", "vS", C["vS"])
        rguard("vector", "scr1")
        op("vector", lambda E, x=xap, r=rstd_ap: E.tensor_scalar(B["scr1"][:], x, r, None, ALU.mult), inc=("vS", 1))
        rmark("scr2", "vS")
        rguard("vector", "scr2")
        op("vector", lambda E, n=nmr_ap: E.tensor_scalar(B["scr2"][:], B["scr1"][:], n, None, ALU.add), inc=("vS", 1))
        rmark("scr1", "vS")
        rguard("vector", "scr1")
        op("vector", lambda E: E.tensor_tensor(B["scr1"][:], B["scr2"][:], B["bc1"][:], ALU.mult))
        rmark("scr2", "vS")
        if outname is not None:
            rguard("vector", outname)
        op("vector", lambda E, o=outap: E.tensor_tensor(o, B["scr1"][:], B["bc0"][:], ALU.add), inc=("vS", 1))
        rmark("scr1", "vS")
        return C["vS"]

    bw("vector")
    bw("scalar")
    for j in range(NBKV):
        sl = xs[j % 2]
        sl.load(lambda j=j: B["xblk"][:, j % 2, :],
                lambda j=j: dram["xkv"][j * 128:(j + 1) * 128, :],
                extra_wait=BAR() if j < 2 else None)
        sl.wait("scalar")
        sl.wait("vector")
        h1v = ln_adaln(B["xblk"][:, j % 2, :], B["rstd1"][:, j:j + 1], B["nmr1"][:, j:j + 1],
                       B["dump"][:], "dump")
        sl.set_free("vS", h1v)
        W("tensor", "vS", h1v)
        transpose16(lambda c: B["dump"][:, c * 128:(c + 1) * 128],
                    lambda c4, j=j: B["h1Tb"][:, j % 2, c4 * 512:(c4 + 1) * 512].rearrange("p (a t) -> p a t", a=4),
                    "identb", mark="dump")
        W("tensor", "aS", C["aS"])
        bguard(0)
        for kvh in range(4):
            for cht in range(16):
                op("tensor", lambda E, k=kvh, c=cht, j=j: E.matmul(
                    ps[0][:, k * 128:(k + 1) * 128],
                    B["WkT"][:, c, k * 128:(k + 1) * 128],
                    B["h1Tb"][:, j % 2, c * 128:(c + 1) * 128],
                    start=(c == 0), stop=(c == 15)), inc=("tS", 1) if cht == 15 else None)
        kps_t = C["tS"]
        bguard(1)
        for cht in range(16):
            op("tensor", lambda E, c=cht, j=j: E.matmul(
                ps[1][:, :], B["h1Tb"][:, j % 2, c * 128:(c + 1) * 128],
                B["WvT"][:, c, :], start=(c == 0), stop=(c == 15)),
               inc=("tS", 1) if cht == 15 else None)
        vps_t = C["tS"]
        W("scalar", "tS", kps_t)
        rguard("scalar", "dump")
        op("scalar", lambda E: E.activation(B["dump"][:, 0:512], ps[0][:, :], AF.Square), inc=("aS", 1))
        W("tensor", "aS", C["aS"])
        bguard(3)
        for kvh in range(4):
            op("tensor", lambda E, k=kvh: E.matmul(
                ps[3][0:1, k * 128:(k + 1) * 128], B["onescol"][:],
                B["dump"][:, k * 128:(k + 1) * 128], start=True, stop=True),
               inc=("tS", 1) if kvh == 3 else None)
        rmark("dump", "tS")
        W("vector", "tS", C["tS"])
        op("vector", lambda E: E.tensor_scalar(B["r1a"][:], ps[3][0:1, :], 1.0 / DH, None, ALU.mult), inc=("vS", 1))
        bread(3, "vS")
        op("vector", lambda E: E.tensor_scalar(B["r1a"][:], B["r1a"][:], EPS2, None, ALU.add), inc=("vS", 1))
        W("scalar", "vS", C["vS"])
        op("scalar", lambda E: E.activation(B["r1b"][:], B["r1a"][:], AF.Sqrt), inc=("aS", 1))
        W("vector", "aS", C["aS"])
        op("vector", lambda E: E.reciprocal(B["r1c"][:], B["r1b"][:]), inc=("vS", 1))
        W("vector", "vS", consts_v)
        rguard("vector", "k1")
        op("vector", lambda E: E.tensor_scalar(B["k1"][:], ps[0][:, :], B["kn1"][:], None, ALU.mult), inc=("vS", 1))
        bread(0, "vS")
        k1_v = C["vS"]
        rguard("gpsimd", "ksw")
        W("gpsimd", "vS", k1_v)
        sw.load(lambda: B["ksw"][0:64, :], lambda: B["k1"][64:128, :])
        sw.load(lambda: B["ksw"][64:128, :], lambda: B["k1"][0:64, :])
        cs.wait("vector")
        rguard("vector", "kc1"); rguard("vector", "kc2")
        for kvh in range(4):
            op("vector", lambda E, k=kvh, j=j: E.tensor_tensor(
                B["kc1"][:, k * 128:(k + 1) * 128], B["k1"][:, k * 128:(k + 1) * 128],
                B["cosF"][:, j * 128:(j + 1) * 128], ALU.mult))
        sw.wait("vector")
        for kvh in range(4):
            op("vector", lambda E, k=kvh, j=j: E.tensor_tensor(
                B["kc2"][:, k * 128:(k + 1) * 128], B["ksw"][:, k * 128:(k + 1) * 128],
                B["sinF"][:, j * 128:(j + 1) * 128], ALU.mult))
        op("vector", lambda E: E.tensor_tensor(B["kc1"][:], B["kc1"][:], B["kc2"][:], ALU.add), inc=("vS", 1))
        rmark("k1", "vS"); rmark("ksw", "vS")
        W("tensor", "vS", C["vS"])
        bguard(3)
        op("tensor", lambda E: E.matmul(ps[3][:, :], B["onesrow"][:], B["r1c"][:], start=True, stop=True), inc=("tS", 1))
        W("vector", "tS", C["tS"])
        op("vector", lambda E, j=j: E.tensor_tensor(
            B["kC"][:, :, j * 128:(j + 1) * 128],
            B["kc1"][:].rearrange("p (h t) -> p h t", h=4),
            ps[3][:, :].rearrange("p (h t) -> p h t", h=4), ALU.mult), inc=("vS", 1))
        bread(3, "vS")
        rmark("kc1", "vS")
        W("scalar", "tS", vps_t)
        op("scalar", lambda E, j=j: E.copy(B["vT"][:, j, :], ps[1][:, :]), inc=("aS", 1))
        bread(1, "aS")
        xs[j % 2].set_free("aS", C["aS"])
    free("kc2", "kc1", "ksw", "k1", "h1Tb", "xblk", "sinF", "cosF", "WvT", "WkT")
    p1_t, p1_v, p1_a = C["tS"], C["vS"], C["aS"]
    cut(2)

    csq = Slot("sync")
    wqs = Slot("gpsimd")
    swq = Slot("gpsimd")
    xq_s = Slot("sync")
    x2w = Slot("sync")

    for half in range(2):
        if half == 1:
            bcast("bc0", 0)
            bcast("bc1", 1)
        H1 = f"h1Th{half}"
        alloc(H1, (128, 16, 1024), BF16)
        alloc("xqr" + str(half), (128, D), F32)
        XQ = "xqr" + str(half)
        bw("vector")
        bw("scalar")
        for tb in range(8):
            j = 4 + half * 8 + tb
            xs[0].load(lambda j=j, X=XQ: B[X][:], lambda j=j: dram["xkv"][j * 128:(j + 1) * 128, :],
                       extra_wait=(BAR() + [("vS", C["vS"]), ("aS", C["aS"])]) if tb == 0 else None)
            xs[0].wait("vector")
            W("vector", "vS", p1_v)
            rguard("vector", "scr1")
            op("vector", lambda E, j=j, X=XQ: E.tensor_scalar(B["scr1"][:], B[X][:], B["rstd1"][:, j:j + 1], None, ALU.mult), inc=("vS", 1))
            W("vector", "vS", C["vS"])
            op("vector", lambda E, j=j: E.tensor_scalar(B["scr1"][:], B["scr1"][:], B["nmr1"][:, j:j + 1], None, ALU.add), inc=("vS", 1))
            xs[0].set_free("vS", C["vS"])
            rguard("vector", "scr2")
            op("vector", lambda E: E.tensor_tensor(B["scr2"][:], B["scr1"][:], B["bc1"][:], ALU.mult))
            rmark("scr1", "vS")
            op("vector", lambda E: E.tensor_tensor(B["scr2"][:], B["scr2"][:], B["bc0"][:], ALU.add), inc=("vS", 1))
            W("scalar", "vS", C["vS"])
            rguard("scalar", "dump")
            op("scalar", lambda E: E.copy(B["dump"][:], B["scr2"][:]), inc=("aS", 1))
            rmark("scr2", "aS")
            W("tensor", "aS", C["aS"])
            transpose16(lambda c: B["dump"][:, c * 128:(c + 1) * 128],
                        lambda c4, tb=tb, H=H1: B[H][:, c4 * 4:(c4 + 1) * 4, tb * 128:(tb + 1) * 128],
                        "identb", mark="dump")
        free(XQ)
        h1Th_a = C["aS"]
        if half == 0 and KCUT == 21:
            dbg = Slot("sync")
            ew = [("aS", C["aS"]), ("vS", C["vS"]), ("tS", C["tS"])]
            for sn, cv in list(C.items()):
                if sn.startswith("dq"):
                    ew.append((sn, cv))
            dbg.load(lambda: out[0:128, 0:2048], lambda: B["scr2"][:, 0:2048], extra_wait=ew)
            dbg.load(lambda: out[128:256, 0:20], lambda: B["rstd1"][:, 0:20])
            dbg.load(lambda: out[256:384, 0:20], lambda: B["nmr1"][:, 0:20])
            em("sync", lambda E, d=dbg, v=16 * dbg.uses: E.wait_ge(sems[d.s], v))
            _do_replay(nc, cmds, stack)
            e = _Cut(); e.nc = nc
            raise e
        rguard("sync", "scr1")
        cosq = B["scr1"][:, 0:1024]
        sinq = B["scr1"][:, 1024:2048]
        csq.load(lambda h=half: B["scr1"][:, 0:1024],
                 lambda h=half: dram["cosT"][:, PRE + h * 1024: PRE + (h + 1) * 1024])
        csq.load(lambda h=half: B["scr1"][:, 1024:2048],
                 lambda h=half: dram["sinT"][:, PRE + h * 1024: PRE + (h + 1) * 1024])

        bw("vector")
        bw("scalar")
        alloc(f"ctxC{half}", (128, 16, 1024), BF16)
        alloc(f"WqTh{half}", (128, 16, 128), BF16)
        alloc(f"wqn{half}", (128, D), BF16)
        alloc(f"qCg{half}", (128, 4, 1024), BF16)
        alloc(f"attb{half}", (128, 5, 512), BF16)
        alloc(f"q1{half}", (128, 512), F32)
        alloc(f"qsw{half}", (128, 512), F32)
        alloc(f"qc1{half}", (128, 512), F32)
        alloc(f"bcq{half}", (128, 512), F32)
        WQ, WN, QC, AT, Q1, QS, QA, BQ, CX = (f"WqTh{half}", f"wqn{half}", f"qCg{half}",
            f"attb{half}", f"q1{half}", f"qsw{half}", f"qc1{half}", f"bcq{half}", f"ctxC{half}")

        for grp in range(4):
            kvh = grp
            for h4 in range(4):
                h = grp * 4 + h4
                rguard("gpsimd", WN)
                wqs.load(lambda W_=WN: B[W_][:], lambda h=h: dram["wq"][h * 128:(h + 1) * 128, :],
                         extra_wait=BAR() if (grp == 0 and h4 == 0) else None)
                wqs.wait("tensor")
                W("tensor", "aS", h1Th_a)
                rguard("scalar", WQ)
                wqT_a = transpose16(lambda c, W_=WN: B[W_][:, c * 128:(c + 1) * 128],
                                    lambda c4, W_=WQ: B[W_][:, c4 * 4:(c4 + 1) * 4, :],
                                    "identb", mark=WN)
                for nch in range(2):
                    W("tensor", "aS", wqT_a)
                    bguard(0)
                    for cht in range(16):
                        op("tensor", lambda E, c=cht, n=nch, W_=WQ, H=H1: E.matmul(
                            ps[0][:, :], B[W_][:, c, :], B[H][:, c, n * 512:(n + 1) * 512],
                            start=(c == 0), stop=(c == 15)), inc=("tS", 1) if cht == 15 else None)
                    qp_t = C["tS"]
                    rmark(WQ, "tS")
                    W("scalar", "tS", qp_t)
                    rguard("scalar", "dump")
                    op("scalar", lambda E: E.activation(B["dump"][:, 0:512], ps[0][:, :], AF.Square), inc=("aS", 1))
                    W("tensor", "aS", C["aS"])
                    bguard(3)
                    op("tensor", lambda E: E.matmul(ps[3][0:1, :], B["onescol"][:], B["dump"][:, 0:512],
                                                    start=True, stop=True), inc=("tS", 1))
                    rmark("dump", "tS")
                    W("vector", "tS", C["tS"])
                    op("vector", lambda E: E.tensor_scalar(B["r1a"][:], ps[3][0:1, :], 1.0 / DH, None, ALU.mult), inc=("vS", 1))
                    bread(3, "vS")
                    op("vector", lambda E: E.tensor_scalar(B["r1a"][:], B["r1a"][:], EPS2, None, ALU.add), inc=("vS", 1))
                    W("scalar", "vS", C["vS"])
                    op("scalar", lambda E: E.activation(B["r1b"][:], B["r1a"][:], AF.Sqrt), inc=("aS", 1))
                    W("vector", "aS", C["aS"])
                    op("vector", lambda E: E.reciprocal(B["r1c"][:], B["r1b"][:]), inc=("vS", 1))
                    W("tensor", "vS", C["vS"])
                    bguard(3)
                    op("tensor", lambda E: E.matmul(ps[3][:, :], B["qn1s"][:], B["r1c"][:], start=True, stop=True), inc=("tS", 1))
                    W("scalar", "tS", C["tS"])
                    rguard("scalar", BQ)
                    op("scalar", lambda E, b=BQ: E.copy(B[b][:], ps[3][:, :]), inc=("aS", 1))
                    bread(3, "aS")
                    W("vector", "aS", C["aS"])
                    rguard("vector", Q1)
                    op("vector", lambda E, b=BQ, q=Q1: E.tensor_tensor(B[q][:], ps[0][:, :], B[b][:], ALU.mult), inc=("vS", 1))
                    bread(0, "vS")
                    rmark(BQ, "vS")
                    q1_v = C["vS"]
                    rguard("gpsimd", QS)
                    W("gpsimd", "vS", q1_v)
                    swq.load(lambda q=QS: B[q][0:64, :], lambda q=Q1: B[q][64:128, :])
                    swq.load(lambda q=QS: B[q][64:128, :], lambda q=Q1: B[q][0:64, :])
                    csq.wait("vector")
                    rguard("vector", QA)
                    op("vector", lambda E, n=nch, q=Q1, a=QA: E.tensor_tensor(
                        B[a][:], B[q][:], B["scr1"][:, n * 512:(n + 1) * 512], ALU.mult))
                    rmark(Q1, "vS")
                    swq.wait("vector")
                    op("vector", lambda E, n=nch, q=QS, a=QA, b=Q1: E.tensor_tensor(
                        B[b][:], B[q][:], B["scr1"][:, 1024 + n * 512:1024 + (n + 1) * 512], ALU.mult))
                    rmark(QS, "vS")
                    op("vector", lambda E, h4=h4, n=nch, a=QA, b=Q1, qc=QC: E.tensor_tensor(
                        B[qc][:, h4, n * 512:(n + 1) * 512], B[a][:], B[b][:], ALU.add), inc=("vS", 1))
                    rmark(QA, "vS")
            qprep_v = C["vS"]
            if half == 0 and grp == 0:
                cut(22)
            for qb in range(8):
                Q = half * 8 + qb
                W("tensor", "vS", qprep_v)
                rguard("scalar", AT)
                for e in range(5):
                    kb = Q + e
                    sbank = 0 if e % 2 == 0 else 6
                    bguard(sbank)
                    for qh in range(4):
                        op("tensor", lambda E, e=e, k=kb, q=qh, kv=kvh, qb=qb, b=sbank, qc=QC: E.matmul(
                            ps[b][:, q * 128:(q + 1) * 128],
                            B["kC"][:, kv, k * 128:(k + 1) * 128],
                            B[qc][:, q, qb * 128:(qb + 1) * 128],
                            start=True, stop=True), inc=("tS", 1) if q == 3 else None)
                    sc_t = C["tS"]
                    if e in (0, 4):
                        W("vector", "tS", sc_t)
                        moff = 0 if e == 0 else 512
                        rguard("vector", BQ)
                        op("vector", lambda E, m=moff, b=sbank, bq=BQ: E.tensor_tensor(
                            B[bq][:], ps[b][:, :], B["maskadd"][:, m:m + 512], ALU.add), inc=("vS", 1))
                        bread(sbank, "vS")
                        W("scalar", "vS", C["vS"])
                        op("scalar", lambda E, e=e, k=kb, bq=BQ, a=AT: E.activation(
                            B[a][:, e, :], B[bq][:], AF.Exp, bias=B["kbT"][:, k:k + 1]), inc=("aS", 1))
                        rmark(BQ, "aS")
                    else:
                        W("scalar", "tS", sc_t)
                        op("scalar", lambda E, e=e, k=kb, b=sbank, a=AT: E.activation(
                            B[a][:, e, :], ps[b][:, :], AF.Exp, bias=B["kbT"][:, k:k + 1]), inc=("aS", 1))
                        bread(sbank, "aS")
                att_a = C["aS"]
                W("tensor", "aS", att_a)
                bguard(1); bguard(2)
                for qh in range(4):
                    for e in range(5):
                        kb = Q + e
                        op("tensor", lambda E, e=e, k=kb, q=qh, kv=kvh, a=AT: E.matmul(
                            ps[1][:, q * 128:(q + 1) * 128],
                            B["vT"][:, k, kv * 128:(kv + 1) * 128],
                            B[a][:, e, q * 128:(q + 1) * 128],
                            start=(e == 0), stop=(e == 4)), inc=None)
                for e in range(5):
                    op("tensor", lambda E, e=e, a=AT: E.matmul(
                        ps[2][0:1, :], B["onescol"][:], B[a][:, e, :],
                        start=(e == 0), stop=(e == 4)),
                       inc=("tS", 1) if e == 4 else None)
                ctx_t = C["tS"]
                rmark(AT, "tS")
                W("vector", "tS", ctx_t)
                op("vector", lambda E: E.reciprocal(B["r1c"][:], ps[2][0:1, :]), inc=("vS", 1))
                bread(2, "vS")
                W("tensor", "vS", C["vS"])
                bguard(3)
                op("tensor", lambda E: E.matmul(ps[3][:, :], B["onesrow"][:], B["r1c"][:], start=True, stop=True), inc=("tS", 1))
                W("scalar", "tS", C["tS"])
                rguard("scalar", BQ)
                op("scalar", lambda E, b=BQ: E.copy(B[b][:], ps[3][:, :]), inc=("aS", 1))
                bread(3, "aS")
                W("vector", "aS", C["aS"])
                op("vector", lambda E, g=grp, qb=qb, b=BQ, cx=CX: E.tensor_tensor(
                    B[cx][:, g * 4:(g + 1) * 4, qb * 128:(qb + 1) * 128],
                    ps[1][:, :].rearrange("p (q t) -> p q t", q=4),
                    B[b][:].rearrange("p (q t) -> p q t", q=4),
                    ALU.mult), inc=("vS", 1))
                bread(1, "vS")
                rmark(BQ, "vS")
                if half == 0 and grp == 0 and qb == 0:
                    cut(23, ("sb", BQ, 512))
            if half == 0 and grp == 0:
                cut(24)
        if half == 0:
            cut(25)
        free(BQ, QA, QS, Q1, AT, QC, WN, WQ)
        bcast("bc1", 2)
        ga_a = C["aS"]
        if half == 0:
            cut(26, ("sb", "bc1", 2048))

        alloc(f"WoT{half}", (128, 16, 512), BF16)
        alloc(f"won{half}", (128, D), BF16)
        alloc(f"xh{half}", (128, 512), F32)
        alloc(f"x2h{half}", (128, 512), F32)
        WO, WON, XH, X2H = f"WoT{half}", f"won{half}", f"xh{half}", f"x2h{half}"
        wos = Slot("gpsimd")
        bw("vector")
        bw("scalar")
        for och in range(4):
            for r4 in range(4):
                r = och * 4 + r4
                rguard("gpsimd", WON)
                wos.load(lambda w=WON: B[w][:], lambda r=r: dram["wo"][r * 128:(r + 1) * 128, :],
                         extra_wait=BAR() if (och == 0 and r4 == 0) else None)
                wos.wait("tensor")
                rguard("scalar", WO)
                transpose16(lambda c, w=WON: B[w][:, c * 128:(c + 1) * 128],
                            lambda c4, r4=r4, w=WO: B[w][:, c4 * 4:(c4 + 1) * 4, r4 * 128:(r4 + 1) * 128],
                            "identb", mark=WON)
            woT_a = C["aS"]
            if half == 0 and och == 0:
                cut(31)
            for tb in range(8):
                jq = half * 8 + tb
                xq_s.load(lambda j=jq, o=och, x=XH: B[x][:],
                          lambda j=jq, o=och: dram["xkv"][(4 + j) * 128:(5 + j) * 128, o * 512:(o + 1) * 512],
                          extra_wait=(BAR() + [("vS", C["vS"])]) if (tb == 0 and och == 0) else None)
                W("tensor", "aS", woT_a)
                bguard(0)
                for h in range(16):
                    op("tensor", lambda E, h=h, tb=tb, w=WO, cx=CX: E.matmul(
                        ps[0][:, :], B[cx][:, h, tb * 128:(tb + 1) * 128],
                        B[w][:, h, :],
                        start=(h == 0), stop=(h == 15)),
                       inc=("tS", 1) if h == 15 else None)
                o_t = C["tS"]
                rmark(WO, "tS")
                rmark(CX, "tS")
                W("vector", "tS", o_t)
                W("vector", "aS", ga_a)
                xq_s.wait("vector")
                x2w.wait("vector")
                rguard("vector", X2H)
                rguard("vector", "scr2")
                op("vector", lambda E, o=och: E.tensor_tensor(
                    B["scr2"][:, 0:512], ps[0][:, :], B["bc1"][:, o * 512:(o + 1) * 512], ALU.mult), inc=("vS", 1))
                bread(0, "vS")
                op("vector", lambda E, x=XH, y=X2H: E.tensor_tensor(
                    B[y][:], B["scr2"][:, 0:512], B[x][:], ALU.add), inc=("vS", 1))
                x2_v = C["vS"]
                xq_s.set_free("vS", x2_v)
                x2w.load(lambda j=jq, o=och: x2_sp[j * 128:(j + 1) * 128, o * 512:(o + 1) * 512],
                         lambda y=X2H: B[y][:], extra_wait=("vS", x2_v))
                if half == 0 and och == 0 and tb == 0:
                    cut(32)
                if half == 0 and och == 0 and tb == 3:
                    cut(33)
            if half == 0 and och == 0:
                cut(27)
        if half == 0:
            cut(28)
        free(X2H, XH, WON, WO, CX, H1)
        cut(3, "x2")
    p2_t = C["tS"]
    free("vT", "kC")

    bcast("bc1", 4)
    bcast("bc0", 3)
    cbsb_a = C["aS"]
    alloc("x2rd", (128, 2, D), F32)
    alloc("qx8", (128, D), FP8)
    alloc("qx8T", (128, 16, 128), FP8)
    bw("vector")
    bw("scalar")
    x2r = [Slot("sync"), Slot("sync")]
    q8w = Slot("sync")
    for tb in range(NBQ):
        sl = x2r[tb % 2]
        sl.load(lambda t=tb: B["x2rd"][:, t % 2, :], lambda t=tb: x2_sp[t * 128:(t + 1) * 128, :],
                extra_wait=(BAR() + [(x2w.s, 16 * x2w.uses)]) if tb < 2 else None)
        sl.wait("scalar")
        sl.wait("vector")
        xap = B["x2rd"][:, tb % 2, :]
        W("scalar", "aS", cbsb_a)
        hv = ln_adaln(xap, B["rstd2"][:], B["nmr2"][:], B["scr2"][:], "scr2")
        sl.set_free("vS", hv)
        op("vector", lambda E: E.tensor_reduce(B["s11"][:], B["scr2"][:], AX.X, ALU.max, apply_absolute_value=True))
        op("vector", lambda E: E.tensor_copy(B["hzd"][:], B["s12"][:]))
        op("vector", lambda E, t=tb: E.tensor_scalar(B["rxf"][:, t:t + 1], B["s11"][:], 1.0 / 224.0, None, ALU.mult), inc=("vS", 1))
        W("vector", "vS", C["vS"])
        op("vector", lambda E, t=tb: E.reciprocal(B["s13"][:], B["rxf"][:, t:t + 1]), inc=("vS", 1))
        W("vector", "vS", C["vS"])
        rguard("vector", "qx8")
        op("vector", lambda E: E.tensor_scalar(B["qx8"][:], B["scr2"][:], B["s13"][:], None, ALU.mult), inc=("vS", 1))
        rmark("scr2", "vS")
        qx8_v = C["vS"]
        W("tensor", "vS", qx8_v)
        if tb == 0:
            W("tensor", "tS", p2_t)
        rguard("scalar", "qx8T")
        transpose16(lambda c: B["qx8"][:, c * 128:(c + 1) * 128],
                    lambda c4: B["qx8T"][:, c4 * 4:(c4 + 1) * 4, :],
                    "ident8", mark="qx8")
        q8w.load(lambda t=tb: qx8_sp[t, :, :, :], lambda: B["qx8T"][:],
                 extra_wait=("aS", C["aS"]))
        rmark("qx8T", q8w.s)
    passb_v = C["vS"]
    free("qx8T", "qx8", "x2rd")
    cut(4)

    alloc("wnat2", (128, 2, D), F32)
    alloc("qw8n", (128, D), FP8)
    alloc("q8b", (128, 2, 16, 128), FP8)
    alloc("tblk", (128, 2, 1536), F32)
    alloc("wqg", (128, 16, 1536), FP8)
    alloc("wqu", (128, 16, 1536), FP8)
    alloc("fgB", (128, 1536), F32)
    alloc("fuB", (128, 1536), F32)
    alloc("gub", (128, 512), F32)
    alloc("sgb", (128, 512), F32)
    bw("vector")
    bw("scalar")
    wgl = [Slot("sync"), Slot("sync")]
    q8r = [Slot("gpsimd"), Slot("gpsimd")]
    tw = [Slot("sync"), Slot("sync")]
    bnw = Slot("gpsimd")
    bnr = Slot("gpsimd")

    def quant_transpose_w(src, goff, nrt, dstname, rowname, browoff):
        for rc in range(nrt):
            sl = wgl[rc % 2]
            sl.load(lambda r=rc: B["wnat2"][:, r % 2, :],
                    lambda r=rc, s=src, g=goff: dram[s][g + r * 128:g + (r + 1) * 128, :],
                    extra_wait=BAR() if (goff == 0 and rc < 2 and src == "wg") else None)
            sl.wait("vector")
            op("vector", lambda E, r=rc: E.tensor_reduce(B["s11"][:], B["wnat2"][:, r % 2, :], AX.X, ALU.max, apply_absolute_value=True), inc=("vS", 1))
            op("vector", lambda E: E.tensor_copy(B["hzd"][:], B["s12"][:]))
            rguard("vector", "wmc")
            op("vector", lambda E, r=rc: E.tensor_scalar(B["wmc"][:, r:r + 1], B["s11"][:], 1.0 / 224.0, None, ALU.mult), inc=("vS", 1))
            W("vector", "vS", C["vS"])
            op("vector", lambda E, r=rc: E.reciprocal(B["s13"][:], B["wmc"][:, r:r + 1]), inc=("vS", 1))
            W("vector", "vS", C["vS"])
            rguard("vector", "qw8n")
            op("vector", lambda E, r=rc: E.tensor_scalar(B["qw8n"][:], B["wnat2"][:, r % 2, :], B["s13"][:], None, ALU.mult), inc=("vS", 1))
            sl.set_free("vS", C["vS"])
            W("tensor", "vS", C["vS"])
            rguard("scalar", dstname)
            transpose16(lambda c: B["qw8n"][:, c * 128:(c + 1) * 128],
                        lambda c4, r=rc, d=dstname: B[d][:, c4 * 4:(c4 + 1) * 4, r * 128:(r + 1) * 128],
                        "ident8", mark="qw8n")
        W("tensor", "vS", C["vS"])
        bguard(7)
        op("tensor", lambda E: E.matmul(ps[7][0:16, 0:128], B["wmc"][:, 0:16], B["identf"][:], start=True, stop=True), inc=("tS", 1))
        rmark("wmc", "tS")
        W("scalar", "tS", C["tS"])
        rguard("scalar", "wmr16")
        op("scalar", lambda E: E.copy(B["wmr16"][:], ps[7][0:16, 0:128]), inc=("aS", 1))
        bread(7, "aS")
        W("gpsimd", "aS", C["aS"])
        bnw.load(lambda o=browoff: bounce[0:1, o * 128:(o + 16) * 128].rearrange("p (a b) -> (p a) b", a=16), lambda: B["wmr16"][:])
        rmark("wmr16", bnw.s)
        bnr.load(lambda n=nrt, rn=rowname: B[rn][0:1, 0:n * 128],
                 lambda n=nrt, o=browoff: bounce[0:1, o * 128:(o + n) * 128],
                 extra_wait=(bnw.s, 16 * bnw.uses))

    for grp, (gs, ge) in enumerate(GRPS):
        gn = ge - gs
        nrt = gn // 128
        nck = (gn + 511) // 512
        quant_transpose_w("wg", gs, nrt, "wqg", "rowA", 0)
        quant_transpose_w("wu", gs, nrt, "wqu", "rowB", 16)
        bnr.wait("tensor")
        for ck in range(nck):
            w_ = min(512, gn - ck * 512)
            bguard(6); bguard(7)
            op("tensor", lambda E, c=ck, w=w_: E.matmul(ps[6][:, 0:w], B["onesrow"][:], B["rowA"][0:1, c * 512:c * 512 + w], start=True, stop=True), inc=("tS", 1))
            op("tensor", lambda E, c=ck, w=w_: E.matmul(ps[7][:, 0:w], B["onesrow"][:], B["rowB"][0:1, c * 512:c * 512 + w], start=True, stop=True), inc=("tS", 1))
            rmark("rowA", "tS"); rmark("rowB", "tS")
            W("scalar", "tS", C["tS"])
            rguard("scalar", "fgB"); rguard("scalar", "fuB")
            op("scalar", lambda E, c=ck, w=w_: E.copy(B["fgB"][:, c * 512:c * 512 + w], ps[6][:, 0:w]), inc=("aS", 1))
            op("scalar", lambda E, c=ck, w=w_: E.copy(B["fuB"][:, c * 512:c * 512 + w], ps[7][:, 0:w]), inc=("aS", 1))
            bread(6, "aS"); bread(7, "aS")
        fb_a = C["aS"]
        bnr.set_free("tS", C["tS"])
        for tb in range(NBQ):
            sl = q8r[tb % 2]
            sl.load(lambda t=tb: B["q8b"][:, t % 2, :, :], lambda t=tb: qx8_sp[t, :, :, :],
                    extra_wait=(BAR() + [last_read.pop("qx8T", None)]) if (grp == 0 and tb < 2) else None)
            sl.wait("tensor")
            W("tensor", "aS", fb_a)
            for k in range(nck):
                bguard(k); bguard(3 + k)
            for cht in range(16):
                for k in range(nck):
                    w_ = min(512, gn - k * 512)
                    op("tensor", lambda E, c=cht, k=k, w=w_, t=tb: E.matmul(
                        ps[k][:, 0:w], B["q8b"][:, t % 2, c, :], B["wqg"][:, c, k * 512:k * 512 + w],
                        start=(c == 0), stop=(c == 15)), inc=None)
                    op("tensor", lambda E, c=cht, k=k, w=w_, t=tb: E.matmul(
                        ps[3 + k][:, 0:w], B["q8b"][:, t % 2, c, :], B["wqu"][:, c, k * 512:k * 512 + w],
                        start=(c == 0), stop=(c == 15)),
                       inc=("tS", 1) if (cht == 15 and k == nck - 1) else None)
            gu_t = C["tS"]
            sl.set_free("tS", gu_t)
            rmark("wqg", "tS"); rmark("wqu", "tS")
            W("vector", "tS", gu_t)
            W("vector", "vS", passb_v)
            tw[tb % 2].wait("vector")
            for k in range(nck):
                w_ = min(512, gn - k * 512)
                rguard("vector", "gub")
                op("vector", lambda E, k=k, w=w_, t=tb: E.scalar_tensor_tensor(
                    B["gub"][:, 0:w], ps[k][:, 0:w], B["rxf"][:, t:t + 1], B["fgB"][:, k * 512:k * 512 + w], ALU.mult, ALU.mult), inc=("vS", 1))
                bread(k, "vS")
                W("scalar", "vS", C["vS"])
                rguard("scalar", "sgb")
                op("scalar", lambda E, w=w_: E.activation(B["sgb"][:, 0:w], B["gub"][:, 0:w], AF.Silu), inc=("aS", 1))
                rmark("gub", "aS")
                rguard("vector", "gub")
                op("vector", lambda E, k=k, w=w_, t=tb: E.scalar_tensor_tensor(
                    B["gub"][:, 0:w], ps[3 + k][:, 0:w], B["rxf"][:, t:t + 1], B["fuB"][:, k * 512:k * 512 + w], ALU.mult, ALU.mult), inc=("vS", 1))
                bread(3 + k, "vS")
                W("vector", "aS", C["aS"])
                op("vector", lambda E, k=k, w=w_, t=tb: E.tensor_tensor(
                    B["tblk"][:, t % 2, k * 512:k * 512 + w], B["sgb"][:, 0:w], B["gub"][:, 0:w], ALU.mult), inc=("vS", 1))
                rmark("sgb", "vS")
            op("vector", lambda E, t=tb, g=gn: E.tensor_reduce(B["s11"][:], B["tblk"][:, t % 2, 0:g], AX.X, ALU.max, apply_absolute_value=True), inc=("vS", 1))
            op("vector", lambda E: E.tensor_copy(B["hzd"][:], B["s12"][:]))
            if grp == 0:
                op("vector", lambda E, t=tb: E.tensor_copy(B["rtb"][:, t:t + 1], B["s11"][:]), inc=("vS", 1))
            else:
                src_rt, dst_rt = ("rtb", "rta") if grp % 2 == 1 else ("rta", "rtb")
                op("vector", lambda E, t=tb, s=src_rt, d=dst_rt: E.tensor_tensor(
                    B[d][:, t:t + 1], B[s][:, t:t + 1], B["s11"][:], ALU.max), inc=("vS", 1))
            t_v = C["vS"]
            tw[tb % 2].load(lambda t=tb, g0=gs, g=gn: t_sp[t * 128:(t + 1) * 128, g0:g0 + g],
                            lambda t=tb, g=gn: B["tblk"][:, t % 2, 0:g],
                            extra_wait=("vS", t_v))
    free("sgb", "gub", "fuB", "fgB", "wqu", "wqg", "tblk", "q8b", "qw8n", "wnat2")
    f1_t = C["tS"]
    op("vector", lambda E: E.tensor_scalar(B["rtf"][:], B["rta"][:], 1.0 / 224.0, None, ALU.mult), inc=("vS", 1))
    W("vector", "vS", C["vS"])
    op("vector", lambda E: E.reciprocal(B["rtinv"][:], B["rtf"][:]), inc=("vS", 1))
    rtf_v = C["vS"]
    cut(5)

    alloc("wqd", (128, 44, D), FP8)
    alloc("wnat3", (128, 2, 2816), F32)
    alloc("q8d", (128, IT), FP8)
    bw("vector")
    bw("scalar")
    wdl = [Slot("sync"), Slot("sync")]
    for rc in range(16):
        for hh in range(2):
            wdl[hh].load(lambda r=rc, h=hh: B["wnat3"][:, h, :],
                         lambda r=rc, h=hh: dram["wd"][r * 128:(r + 1) * 128, h * 2816:(h + 1) * 2816],
                         extra_wait=BAR() if rc == 0 else None)
        wdl[0].wait("vector")
        op("vector", lambda E: E.tensor_reduce(B["s11"][:], B["wnat3"][:, 0, :], AX.X, ALU.max, apply_absolute_value=True), inc=("vS", 1))
        wdl[1].wait("vector")
        op("vector", lambda E: E.tensor_reduce(B["s12"][:], B["wnat3"][:, 1, :], AX.X, ALU.max, apply_absolute_value=True))
        op("vector", lambda E: E.tensor_copy(B["hzd"][:], B["s14"][:]))
        op("vector", lambda E: E.tensor_tensor(B["s11"][:], B["s11"][:], B["s12"][:], ALU.max), inc=("vS", 1))
        rguard("vector", "wmc")
        op("vector", lambda E, r=rc: E.tensor_scalar(B["wmc"][:, r:r + 1], B["s11"][:], 1.0 / 224.0, None, ALU.mult), inc=("vS", 1))
        W("vector", "vS", C["vS"])
        op("vector", lambda E, r=rc: E.reciprocal(B["s13"][:], B["wmc"][:, r:r + 1]), inc=("vS", 1))
        W("vector", "vS", C["vS"])
        rguard("vector", "q8d")
        for hh in range(2):
            op("vector", lambda E, h=hh: E.tensor_scalar(B["q8d"][:, h * 2816:(h + 1) * 2816], B["wnat3"][:, h, :], B["s13"][:], None, ALU.mult), inc=("vS", 1))
            wdl[hh].set_free("vS", C["vS"])
        W("tensor", "vS", C["vS"])
        if rc == 0:
            W("tensor", "tS", f1_t)
        for it4 in range(11):
            bk = 4 + (it4 % 2)
            bguard(bk)
            for ii in range(4):
                it = it4 * 4 + ii
                op("tensor", lambda E, t=it, i=ii, b=bk: E.matmul(
                    ps[b][:, i * 128:(i + 1) * 128], B["q8d"][:, t * 128:(t + 1) * 128],
                    B["ident8"][:], start=True, stop=True), inc=("tS", 1) if ii == 3 else None)
            rmark("q8d", "tS")
            W("scalar", "tS", C["tS"])
            rguard("scalar", "wqd")
            op("scalar", lambda E, r=rc, t4=it4, b=bk: E.copy(
                B["wqd"][:, t4 * 4:(t4 + 1) * 4, r * 128:(r + 1) * 128],
                ps[b][:, :].rearrange("p (a t) -> p a t", a=4)), inc=("aS", 1))
            bread(bk, "aS")
    wd_a = C["aS"]
    free("q8d", "wnat3")
    W("tensor", "vS", C["vS"])
    bguard(7)
    op("tensor", lambda E: E.matmul(ps[7][0:16, 0:128], B["wmc"][:, 0:16], B["identf"][:], start=True, stop=True), inc=("tS", 1))
    rmark("wmc", "tS")
    W("scalar", "tS", C["tS"])
    rguard("scalar", "wmr16")
    op("scalar", lambda E: E.copy(B["wmr16"][:], ps[7][0:16, 0:128]), inc=("aS", 1))
    bread(7, "aS")
    W("gpsimd", "aS", C["aS"])
    bnw.load(lambda: bounce[0:1, 32 * 128:48 * 128].rearrange("p (a b) -> (p a) b", a=16), lambda: B["wmr16"][:])
    rguard("sync", "rowA"); rguard("sync", "rowB")
    bnr.load(lambda: B["rowA"][0:1, 0:D], lambda: bounce[0:1, 32 * 128:48 * 128],
             extra_wait=(bnw.s, 16 * bnw.uses))
    rowld.load(lambda: B["rowB"][:], lambda: rows_hbm[5:6, :],
               extra_wait=(rowst.s, 16 * rowst.uses))
    bnr.wait("vector"); rowld.wait("vector")
    op("vector", lambda E: E.tensor_tensor(B["rowA"][:], B["rowA"][:], B["rowB"][:], ALU.mult), inc=("vS", 1))
    gbwd_v = C["vS"]
    rguard("scalar", "bc0")
    for q in range(4):
        bk = 6 + (q % 2)
        bguard(bk)
        W("tensor", "vS", gbwd_v)
        op("tensor", lambda E, q=q, b=bk: E.matmul(
            ps[b][:, :], B["onesrow"][:], B["rowA"][0:1, q * 512:(q + 1) * 512],
            start=True, stop=True), inc=("tS", 1))
        rmark("rowA", "tS")
        W("scalar", "tS", C["tS"])
        op("scalar", lambda E, q=q, b=bk: E.copy(B["bc0"][:, q * 512:(q + 1) * 512], ps[b][:, :]), inc=("aS", 1))
        bread(bk, "aS")
    gbwd_a = C["aS"]
    cut(6)

    alloc("trd", (128, 2, 2816), F32)
    alloc("qt8", (128, IT), FP8)
    alloc("qt8T", (128, 44, 128), FP8)
    alloc("x2f", (128, D), F32)
    alloc("outb", (128, 1024), F32)
    bw("vector")
    bw("scalar")
    trd_s = [Slot("sync"), Slot("sync")]
    x2f_s = Slot("sync")
    outw = Slot("gpsimd")
    for tb in range(NBQ):
        for hh in range(2):
            sl = trd_s[hh]
            sl.load(lambda t=tb, h=hh: B["trd"][:, h, :],
                    lambda t=tb, h=hh: t_sp[t * 128:(t + 1) * 128, h * 2816:(h + 1) * 2816],
                    extra_wait=(BAR() + [(tw[0].s, 16 * tw[0].uses)]) if (tb == 0 and hh == 0) else None)
            if tb == 0 and hh == 0:
                W(sl.queue, tw[1].s, 16 * tw[1].uses)
        W("vector", "vS", rtf_v)
        rguard("vector", "qt8")
        for hh in range(2):
            trd_s[hh].wait("vector")
            op("vector", lambda E, t=tb, h=hh: E.tensor_scalar(
                B["qt8"][:, h * 2816:(h + 1) * 2816], B["trd"][:, h, :], B["rtinv"][:, t:t + 1], None, ALU.mult),
               inc=("vS", 1))
            trd_s[hh].set_free("vS", C["vS"])
        qt_v = C["vS"]
        if tb == 1:
            cut(71)
        W("tensor", "vS", qt_v)
        W("tensor", "aS", wd_a)
        for it4 in range(11):
            bk = 4 + (it4 % 2)
            bguard(bk)
            for ii in range(4):
                it = it4 * 4 + ii
                op("tensor", lambda E, t=it, i=ii, b=bk: E.matmul(
                    ps[b][:, i * 128:(i + 1) * 128], B["qt8"][:, t * 128:(t + 1) * 128],
                    B["ident8"][:], start=True, stop=True), inc=("tS", 1) if ii == 3 else None)
            rmark("qt8", "tS")
            W("scalar", "tS", C["tS"])
            rguard("scalar", "qt8T")
            op("scalar", lambda E, t4=it4, b=bk: E.copy(
                B["qt8T"][:, t4 * 4:(t4 + 1) * 4, :],
                ps[b][:, :].rearrange("p (a t) -> p a t", a=4)), inc=("aS", 1))
            bread(bk, "aS")
        qtT_a = C["aS"]
        if tb == 1:
            cut(72)
        W("tensor", "aS", qtT_a)
        for oc in range(4):
            bguard(oc)
        for it in range(44):
            for oc in range(4):
                op("tensor", lambda E, t=it, oc=oc: E.matmul(
                    ps[oc][:, :], B["qt8T"][:, t, :], B["wqd"][:, t, oc * 512:(oc + 1) * 512],
                    start=(t == 0), stop=(t == 43)),
                   inc=("tS", 1) if (it == 43 and oc == 3) else None)
        y_t = C["tS"]
        rmark("qt8T", "tS")
        if tb == 1:
            cut(73)
        x2f_s.load(lambda t=tb: B["x2f"][:], lambda t=tb: x2_sp[t * 128:(t + 1) * 128, :],
                   extra_wait=BAR() if tb == 0 else None)
        W("vector", "tS", y_t)
        W("vector", "aS", gbwd_a)
        x2f_s.wait("vector")
        for oc in range(4):
            rguard("vector", "scr2")
            op("vector", lambda E, oc=oc, t=tb: E.scalar_tensor_tensor(
                B["scr2"][:, 0:512], ps[oc][:, :], B["rtf"][:, t:t + 1], B["bc0"][:, oc * 512:(oc + 1) * 512], ALU.mult, ALU.mult), inc=("vS", 1))
            bread(oc, "vS")
            if oc % 2 == 0:
                outw.wait("vector")
            op("vector", lambda E, oc=oc: E.tensor_tensor(
                B["outb"][:, (oc % 2) * 512:(oc % 2 + 1) * 512], B["scr2"][:, 0:512],
                B["x2f"][:, oc * 512:(oc + 1) * 512], ALU.add),
               inc=("vS", 1) if oc % 2 == 1 else None)
            if oc % 2 == 1:
                outw.load(lambda t=tb, h=oc // 2: out[t * 128:(t + 1) * 128, h * 1024:(h + 1) * 1024],
                          lambda: B["outb"][:], extra_wait=("vS", C["vS"]))
            if tb == 1 and oc == 0:
                cut(81)
            if tb == 1 and oc == 1:
                cut(82)
            if tb == 1 and oc == 2:
                cut(83)
        ob_v = C["vS"]
        x2f_s.set_free("vS", ob_v)
        em("gpsimd", lambda E, v=16 * outw.uses: E.wait_ge(sems[outw.s], v))
        if tb == 0:
            cut(61)
        if tb == 1:
            cut(74)
        if tb == 2:
            cut(75)
        if tb == 3:
            cut(62)
        if tb == 8:
            cut(63)

    _do_replay(nc, cmds, stack)
    return nc


def _do_replay(nc, cmds, stack):
    with nc.Block() as block:
        block.sync(lambda E: [f(E) for f in cmds["sync"]])
        block.gpsimd(lambda E: [f(E) for f in cmds["gpsimd"]])
        block.tensor(lambda E: [f(E) for f in cmds["tensor"]])
        block.vector(lambda E: [f(E) for f in cmds["vector"]])
        block.scalar(lambda E: [f(E) for f in cmds["scalar"]])
    stack.close()
def _host_inputs(ins):
    x = np.ascontiguousarray(ins["L0_attn_x"])
    di = np.asarray(ins["L0_ffn_down_reorder_index"]).astype(np.int64)
    wg = np.ascontiguousarray(np.asarray(ins["L0_ffn_w_gate"])[di])
    wu = np.ascontiguousarray(np.asarray(ins["L0_ffn_w_up"])[di])
    wd = np.ascontiguousarray(ins["L0_ffn_w_down"])
    wq = np.ascontiguousarray(ins["L0_attn_Wq"]); wk = np.ascontiguousarray(ins["L0_attn_Wk"])
    wv = np.ascontiguousarray(ins["L0_attn_Wv"]); wo = np.ascontiguousarray(ins["L0_attn_Wo"])
    wmod = np.ascontiguousarray(ins["L0_wmod"])
    cvec = np.asarray(ins["L0_c"])
    qn = np.asarray(ins["L0_attn_qn_w"]).astype(np.float32)
    kn = np.asarray(ins["L0_attn_kn_w"]).astype(np.float32)

    identb = np.eye(128).astype(ml_dtypes.bfloat16)
    ident8 = np.eye(128).astype(ml_dtypes.float8_e4m3)
    identf = np.eye(128).astype(np.float32)
    onescol = np.ones((128, 1), ml_dtypes.bfloat16)
    onesrow = np.ones((1, 128), np.float32)
    kk, qq = np.meshgrid(np.arange(128), np.arange(128), indexing="ij")
    mlo = np.where(kk >= qq, 0.0, NEG).astype(np.float32)
    mhi = np.where(kk <= qq, 0.0, NEG).astype(np.float32)
    maskadd = np.concatenate([np.tile(mlo, (1, 4)), np.tile(mhi, (1, 4))], axis=1)
    inv = 1.0 / (10000.0 ** (np.arange(0, DH, 2) / DH))

    maps = []
    for core in range(8):
        b, hf = core // 2, core % 2
        main = x[b, hf * 2048:(hf + 1) * 2048]
        if hf == 0:
            prefix = np.zeros((PRE, D), np.float32)
        else:
            prefix = x[b, 2048 - PRE:2048]
        xkv = np.ascontiguousarray(np.concatenate([prefix, main], axis=0))
        pos = hf * 2048 - PRE + np.arange(NKV)
        ang = pos[:, None].astype(np.float64) * inv[None, :]
        cos = np.cos(ang).astype(np.float32).T
        sin = np.sin(ang).astype(np.float32).T
        cosT = np.concatenate([cos, cos], axis=0)
        sinT = np.concatenate([-sin, sin], axis=0)
        kb = np.zeros((128, NBKV), np.float32)
        if hf == 0:
            kb[:, 0:PRE // 128] = NEG
        maps.append({
            "xkv": xkv, "wq": wq, "wk": wk, "wv": wv, "wo": wo,
            "wg": wg, "wu": wu, "wd": wd, "wmod": wmod,
            "cbt": np.ascontiguousarray(cvec[b].reshape(16, 128).T.astype(np.float32)),
            "qnrow": np.ascontiguousarray(qn.reshape(1, 128)),
            "knT": np.ascontiguousarray(kn.reshape(128, 1)),
            "cosT": np.ascontiguousarray(cosT), "sinT": np.ascontiguousarray(sinT),
            "kbT": kb, "maskadd": np.ascontiguousarray(maskadd),
            "identb": identb, "ident8": ident8, "identf": identf,
            "onescol": onescol, "onesrow": onesrow,
        })
    return maps


def kernel(**ins):
    maps = _host_inputs(ins)
    nc = build_program()
    res = run_bass_kernel_spmd(nc, maps, core_ids=list(range(8)))
    outp = np.empty((4, 4096, 2048), np.float32)
    for core in range(8):
        b, hf = core // 2, core % 2
        outp[b, hf * 2048:(hf + 1) * 2048] = res.results[core]["out"]
    return outp
